# version 10
# baseline (speedup 1.0000x reference)
import numpy as np
import concourse.bass as bass
import concourse.mybir as mybir
from concourse.bass_utils import run_bass_kernel_spmd
from concourse.alu_op_type import AluOpType

F32 = mybir.dt.float32
BF16 = mybir.dt.bfloat16
U32 = mybir.dt.uint32
AF = mybir.ActivationFunctionType
AX = mybir.AxisListType

NCORES = 8
D = 4096
CH = 32
H = 32
HD = 128
NJ = 14336
NB = 56
RANK = 16
EPS = 1e-5
INV_SQRT_D = float(np.float32(1.0) / np.float32(np.sqrt(np.float32(128.0))))
THRESH = 0.001
JG = 16
NGRP = NJ // 128 // JG

ENGINES = ("sync", "scalar", "vector", "tensor", "gpsimd")


class B:
    def __init__(self, nc):
        self.nc = nc
        self.ops = {e: [] for e in ENGINES}
        self.sems = {}
        self.cnt = {}
        self.ctxs = []

    def sem(self, name):
        if name in self.sems:
            return
        h = self.nc.semaphore(name)
        self.sems[name] = h.__enter__()
        self.ctxs.append(h)
        self.cnt[name] = 0

    def emit(self, eng, f):
        self.ops[eng].append(f)

    def ins(self, eng, build_fn, inc=None):
        tgt = None
        if inc is not None:
            s, k = inc
            self.sem(s)
            self.cnt[s] += k
            tgt = self.cnt[s]

        def f(e, build_fn=build_fn, inc=inc):
            h = build_fn(e)
            if inc is not None:
                h.then_inc(self.sems[inc[0]], inc[1])
        self.emit(eng, f)
        return tgt

    def wait(self, eng, sem, tgt):
        if tgt is None or tgt <= 0:
            return
        self.sem(sem)
        self.emit(eng, lambda e, s=sem, t=tgt: e.wait_ge(self.sems[s], t))

    def dma(self, queue, out, in_, sem):
        self.sem(sem)
        return self.ins(queue, lambda e, o=out, i=in_: e.dma_start(out=o, in_=i),
                        inc=(sem, 16))

    def barrier(self, skip=()):
        snap = {s: self.cnt[s] for s in self.sems}
        for eng in ENGINES:
            if eng in skip:
                continue
            for s, t in snap.items():
                if t > 0:
                    self.wait(eng, s, t)


def build_program(T=1024, phases=("ffn1", "attn", "ffn2")):
    NTT = T // 128
    SEG = max(1, T // 512)
    SEGW = min(512, T)
    nc = bass.Bass("TRN2", target_bir_lowering=False)

    def din(name, shape, dtype=F32):
        return nc.dram_tensor(name, shape, dtype, kind="ExternalInput").ap()

    x_fm = din("x", [D, T])
    wq = din("wq", [CH, 128, CH * 128])
    wk = din("wk", [CH, 128, CH * 128])
    wv = din("wv", [CH, 128, CH * 128])
    wo = din("wo", [CH, 128, CH * 128])
    bq = din("bq", [128, CH])
    bk = din("bk", [128, CH])
    bv = din("bv", [128, CH])
    bo = din("bo", [128, CH])
    w1t = din("w1t", [28, 128, CH * 512])
    w2g = din("w2g", [NGRP, CH, 128, JG * 128])
    c1t = din("c1t", [128, CH * RANK])
    c2t = din("c2t", [RANK, NJ])
    out_fm = nc.dram_tensor("out", [D, T], F32, kind="ExternalOutput").ap()

    s_res = nc.dram_tensor("s_res", [128, CH * T], BF16).ap()
    s_q = nc.dram_tensor("s_q", [128, CH * T], BF16).ap()
    s_k = nc.dram_tensor("s_k", [128, CH * T], BF16).ap()
    s_v = nc.dram_tensor("s_v", [128, CH * T], BF16).ap()
    s_at = nc.dram_tensor("s_at", [128, (T // 4) * H], BF16).ap()
    s_rows = nc.dram_tensor("s_rows", [2 * NB, T], BF16).ap()
    s_ln = nc.dram_tensor("s_ln", [2, T], F32).ap()
    s_r = nc.dram_tensor("s_r", [128, T // 4], F32).ap()

    import contextlib
    with contextlib.ExitStack() as ctx:
        def sb(name, shape, dtp):
            return ctx.enter_context(nc.sbuf_tensor(name, shape, dtp))

        TA = sb("TA", [128, CH * T], BF16)
        TB = sb("TB", [128, CH * T], BF16)
        ARENA = sb("ARENA", [128, 8192], BF16)
        AR2 = sb("AR2", [128, 4096], BF16)
        AR3 = sb("AR3", [128, 4096], BF16)
        xstage = sb("xstage", [128, 2 * T], F32)
        tstage = sb("tstage", [128, 2 * T], BF16)
        repm = sb("repm", [128, T], BF16)
        repr_ = sb("repr", [128, T], BF16)
        rrep = sb("rrep", [128, 2 * 512], F32)
        rowsA = sb("rowsA", [128, 2 * T], BF16)
        rowsB = sb("rowsB", [128, 2 * T], BF16)
        lnbf = sb("lnbf", [1, 2 * T], BF16)
        onesrow = sb("onesrow", [1, 128], BF16)
        c1sb = sb("c1sb", [128, CH * RANK], BF16)
        zsb = sb("zsb", [RANK, T], BF16)
        idx_all = sb("idx_all", [128, NTT * NB], F32)
        h_all = sb("h_all", [128, NTT * NB], F32)
        mx8 = sb("mx8", [128, 8], F32)
        ix8u = sb("ix8u", [128, 8], U32)
        ix8f = sb("ix8f", [128, 8], F32)
        iota8 = sb("iota8", [128, 8], F32)
        iota256 = sb("iota256", [128, 256], F32)
        iotac = sb("iotac", [128, 2], F32)
        ident = sb("ident", [128, 128], F32)
        onesf = sb("onesf", [128, 1], F32)
        onesb = sb("onesb", [128, 1], BF16)
        epscol = sb("epscol", [1, 1], F32)
        biases = sb("biases", [128, 4 * CH], F32)
        rtile = sb("rtile", [128, T // 4], F32)

        rowstage = AR2[0:64, 0:T]
        junk = AR3[:, 0:2048]
        w1sl = sb("w1sl", [128, 2 * CH * 512], BF16) if CH * T < 2 * CH * 512 else None

        PS = ctx.enter_context(nc.psum_tensor("PS", [128, 4096], F32))
        ps_z = PS[0:RANK, 0:T]
        ps_stx = PS[0:1, 2048:2048 + T]
        ps_stq = PS[0:1, 3072:3072 + T]
        OSL = max(T, 512)
        ps_tr = PS[0:64, 0:128]

        b = B(nc)
        out_tgts = {}
        ev_st = {}

        blk = ctx.enter_context(nc.Block())

        b.ins("gpsimd", lambda e: e.iota(iota256[:, :], [[1, 256]], channel_multiplier=0,
                                         allow_small_or_imprecise_dtypes=True))
        b.ins("gpsimd", lambda e: e.iota(iota8[:, :], [[256, 8]], channel_multiplier=0,
                                         allow_small_or_imprecise_dtypes=True))
        b.ins("gpsimd", lambda e: e.iota(iotac[:, 0:1], [[1, 1]], channel_multiplier=1,
                                         allow_small_or_imprecise_dtypes=True))
        b.ins("gpsimd", lambda e: e.iota(iotac[:, 1:2], [[1, 1]], base=128, channel_multiplier=1,
                                         allow_small_or_imprecise_dtypes=True))
        b.ins("gpsimd", lambda e: e.iota(ident[:, :], [[1, 128]], channel_multiplier=0,
                                         allow_small_or_imprecise_dtypes=True))
        b.ins("gpsimd", lambda e: e.memset(onesf[:, :], 1.0))
        b.ins("gpsimd", lambda e: e.memset(onesb[:, :], 1.0))
        b.ins("gpsimd", lambda e: e.memset(onesrow[:, :], 1.0))
        gp0 = b.ins("gpsimd", lambda e: e.memset(epscol[:, :], EPS), inc=("sg", 1))
        b.dma("gpsimd", c1sb[:, :], c1t, "sw")
        b.dma("sync", biases[:, 0 * CH:1 * CH], bq, "sx")
        b.dma("sync", biases[:, 1 * CH:2 * CH], bk, "sx")
        b.dma("sync", biases[:, 2 * CH:3 * CH], bv, "sx")
        b.dma("sync", biases[:, 3 * CH:4 * CH], bo, "sx")
        b.wait("vector", "sg", gp0)
        b.ins("vector", lambda e: e.tensor_scalar(ident[:, :], ident[:, :], iotac[:, 0:1],
                                                  None, AluOpType.is_equal), inc=("sv", 1))
        b.barrier()

        state = {"res_in_x": True}

        def res_src(m):
            if state["res_in_x"]:
                return x_fm[m * 128:(m + 1) * 128, :]
            return s_res[:, m * T:(m + 1) * T]

        def ln_phase(src_tile, dst_tile, src_is_x):
            ld1 = {}
            cons_ms = {}
            cvt = {}
            if src_is_x:
                for c in range(CH):
                    if c % 2 == 0:
                        ld1[c] = b.dma("gpsimd", dst_tile[:, c * T:(c + 1) * T],
                                       x_fm[c * 128:(c + 1) * 128, :], "lx0")
                    else:
                        xsl = xstage[:, ((c // 2) % 2) * T:((c // 2) % 2) * T + T]
                        if c >= 4:
                            b.wait("sync", "sv", cvt[c - 4])
                        xt_ = b.dma("sync", xsl, x_fm[c * 128:(c + 1) * 128, :], "lx1")
                        b.wait("vector", "lx1", xt_)
                        cvt[c] = b.ins("vector", lambda e, d=dst_tile[:, c * T:(c + 1) * T],
                                       s=xsl: e.tensor_copy(d, s), inc=("sv", 1))
                src_tile = dst_tile
            for c in range(CH):
                if src_is_x:
                    if c % 2 == 0:
                        b.wait("scalar", "lx0", ld1[c])
                    else:
                        b.wait("scalar", "sv", cvt[c])
                src_ap = src_tile[:, c * T:(c + 1) * T]
                sq_ap = tstage[:, (c % 2) * T:(c % 2) * T + T]
                if c >= 2:
                    b.wait("scalar", "st", cons_ms[c - 2])
                ac = b.ins("scalar", lambda e, s=src_ap, d=sq_ap: e.activation(
                    out=d, in_=s, func=AF.Square), inc=("sa", 1))
                b.wait("tensor", "sa", ac)
                last = None
                for g in range(SEG):
                    sl = slice(g * SEGW, g * SEGW + SEGW)
                    b.ins("tensor", lambda e, s=src_ap, sl=sl, c=c: e.matmul(
                        ps_stx[0:1, sl], onesb[:, :], s[:, sl],
                        start=(c == 0), stop=(c == CH - 1)))
                    last = b.ins("tensor", lambda e, s=sq_ap, sl=sl, c=c, g=g: e.matmul(
                        ps_stq[0:1, sl], onesb[:, :], s[:, sl],
                        start=(c == 0), stop=(c == CH - 1)),
                        inc=("st", 1))
                cons_ms[c] = last
            stats_ms = cons_ms[CH - 1]
            M_ = lnbf[0:1, 0:T]; RS = lnbf[0:1, T:2 * T]
            V_ = xstage[0:1, 0:T]; T2_ = xstage[0:1, T:2 * T]
            b.wait("vector", "st", stats_ms)
            v1 = b.ins("vector", lambda e: e.tensor_scalar(
                M_, ps_stx[0:1, :], 1.0 / D, None, AluOpType.mult), inc=("sv", 1))
            b.wait("vector", "sv", v1)
            v2 = b.ins("vector", lambda e: e.tensor_tensor(T2_, M_, M_, AluOpType.mult),
                       inc=("sv", 1))
            v3 = b.ins("vector", lambda e: e.tensor_scalar(
                V_, ps_stq[0:1, :], 1.0 / D, None, AluOpType.mult), inc=("sv", 1))
            b.wait("vector", "sv", v3)
            v4 = b.ins("vector", lambda e: e.tensor_tensor(V_, V_, T2_, AluOpType.subtract),
                       inc=("sv", 1))
            b.wait("scalar", "sv", v4)
            a1 = b.ins("scalar", lambda e: e.activation(out=T2_, in_=V_, func=AF.Sqrt,
                                                        bias=epscol[0:1, 0:1]), inc=("sa", 1))
            b.wait("vector", "sa", a1)

            def _recip_bf(e):
                with nc.allow_low_precision(reason="rstd broadcast row kept in bf16"):
                    return e.reciprocal(RS, T2_)
            v5 = b.ins("vector", _recip_bf, inc=("sv", 1))
            b.wait("tensor", "sv", v1)
            b.ins("tensor", lambda e: e.matmul(
                PS[:, 0:512], onesrow[0:1, :], M_[0:1, 0:512], start=True, stop=True))
            bm1 = b.ins("tensor", lambda e: e.matmul(
                PS[:, 512:1024], onesrow[0:1, :], M_[0:1, 512:1024], start=True, stop=True),
                inc=("st", 1))
            b.wait("tensor", "sv", v5)
            b.ins("tensor", lambda e: e.matmul(
                PS[:, 1024:1536], onesrow[0:1, :], RS[0:1, 0:512], start=True, stop=True))
            bm2 = b.ins("tensor", lambda e: e.matmul(
                PS[:, 1536:2048], onesrow[0:1, :], RS[0:1, 512:1024], start=True, stop=True),
                inc=("st", 1))
            b.wait("vector", "st", bm1)
            e_m = b.ins("vector", lambda e: e.tensor_copy(repm[:, :], PS[:, 0:T]),
                        inc=("sv", 1))
            b.wait("vector", "st", bm2)
            e_r = b.ins("vector", lambda e: e.tensor_copy(repr_[:, :], PS[:, 1024:1024 + T]),
                        inc=("sv", 1))
            vms = {}
            w1s = {}
            slots = (tstage[:, 0:T], tstage[:, T:2 * T], junk[:, 0:T], junk[:, T:2 * T])
            b.wait("vector", "sv", e_r)
            for c in range(CH + 1):
                if c < CH:
                    src_ap = src_tile[:, c * T:(c + 1) * T]
                    tmp = slots[c % 4]
                    if c >= 4:
                        b.wait("vector", "sv", vms[c - 4])
                    w1s[c] = b.ins("vector", lambda e, s=src_ap, d=tmp: e.tensor_tensor(
                        d, s, repm[:, :], AluOpType.subtract), inc=("sv", 1))
                if c >= 1:
                    cc = c - 1
                    b.wait("vector", "sv", w1s[cc])
                    vms[cc] = b.ins("vector", lambda e, d=dst_tile[:, cc * T:(cc + 1) * T],
                                    s=slots[cc % 4]: e.tensor_tensor(
                        d, s, repr_[:, :], AluOpType.mult), inc=("sv", 1))
            b.barrier(skip=("tensor",))
            return vms

        def ffn_phase(xln, work, final, xln_ready=None):
            NSLAB = 28
            sl_area = work if w1sl is None else w1sl[:, :]
            slab_tgts = {}
            it_sel = {}
            NTG = (NTT + 3) // 4

            def w1_dma(s_):
                off = (s_ % 2) * (CH * 512)
                if s_ >= 2:
                    b.wait("gpsimd", "sv", it_sel[(s_ - 2) * 4 + 3])
                slab_tgts[s_] = b.dma("gpsimd", sl_area[:, off:off + CH * 512], w1t[s_],
                                      "w1%d" % (s_ % 2))

            zlast = None
            for k in range(CH):
                if xln_ready is not None:
                    b.wait("tensor", "sv", xln_ready[k])
                for g in range(SEG):
                    sl = slice(g * SEGW, g * SEGW + SEGW)
                    zlast = b.ins("tensor", lambda e, k=k, sl=sl: e.matmul(
                        ps_z[:, sl], c1sb[:, k * RANK:(k + 1) * RANK],
                        xln[:, k * T:(k + 1) * T][:, sl],
                        start=(k == 0), stop=(k == CH - 1)),
                        inc=("st", 1) if (k == CH - 1 and g == SEG - 1) else None)
            b.wait("vector", "st", zlast)
            zev = b.ins("vector", lambda e: e.tensor_copy(zsb[:, :], ps_z[:, :]), inc=("sv", 1))
            b.wait("tensor", "sv", zev)
            NSTRIP = NJ // 2048
            c2_tgts = {}
            strip_done = {}
            c2buf = (AR2[0:RANK, 0:2048], AR2[0:RANK, 2048:4096],
                     AR3[0:RANK, 2048:4096], rowsA[0:RANK, 0:2048])

            def c2_dma(s_):
                if s_ >= 4:
                    b.wait("gpsimd", "st", strip_done[s_ - 4])
                c2_tgts[s_] = b.dma("gpsimd", c2buf[s_ % 4],
                                    c2t[:, s_ * 2048:(s_ + 1) * 2048], "c2%d" % (s_ % 4))

            for _s in range(4):
                c2_dma(_s)
            w1_dma(0)
            w1_dma(1)
            sel_ms = {}
            m2_t = {}
            m3_t = {}
            mm_t = {}

            def sel_mm(s_, tt):
                key = s_ * NTT + tt
                if key == 0:
                    b.wait("tensor", "sv", zev)
                b.wait("tensor", "c2%d" % (s_ % 4), c2_tgts[s_])
                if key >= 1:
                    b.wait("tensor", "sv", m2_t[key - 1])
                ms = None
                for q4 in range(4):
                    ms = b.ins("tensor", lambda e, tt=tt, s_=s_, q4=q4: e.matmul(
                        PS[:, 2048 + q4 * 512:2048 + q4 * 512 + 512],
                        zsb[:, tt * 128:(tt + 1) * 128],
                        c2buf[s_ % 4][:, q4 * 512:q4 * 512 + 512],
                        start=True, stop=True),
                        inc=("st", 1) if q4 == 3 else None)
                if tt == NTT - 1:
                    strip_done[s_] = ms
                    if s_ + 4 < NSTRIP:
                        c2_dma(s_ + 4)
                mm_t[key] = ms

            def sel_dve(s_, tt):
                key = s_ * NTT + tt
                b.wait("vector", "st", mm_t[key])
                s3 = PS[:, 2048:4096].rearrange("p (w j) -> p w j", j=256)
                m1 = b.ins("vector", lambda e, s3=s3: e.reduce_max(mx8[:, :], s3, axis=AX.X),
                           inc=("sv", 1))
                b.wait("vector", "sv", m1)
                m2 = b.ins("vector", lambda e: e.max_index(
                    ix8u[:, :], mx8[:, :], PS[:, 2048:4096]), inc=("sv", 1))
                m2_t[key] = m2
                b.wait("vector", "sv", m2)
                if key >= 1:
                    b.wait("vector", "sg", sel_ms[key - 1])
                m3 = b.ins("vector", lambda e: e.tensor_copy(ix8f[:, :], ix8u[:, :]),
                           inc=("sv", 1))
                m3_t[key] = m3
                b.wait("gpsimd", "sv", m3)
                dst = idx_all[:, tt * NB + s_ * 8:tt * NB + s_ * 8 + 8]
                m4 = b.ins("gpsimd", lambda e, dst=dst: e.tensor_tensor(
                    dst, ix8f[:, :], iota8[:, :], AluOpType.subtract), inc=("sg", 1))
                sel_ms[key] = m4

            w2_tgts = {}
            w2_use_ms = {}
            NQI = NGRP * CH

            def w2_dma(qidx):
                g_, m_ = qidx // CH, qidx % CH
                slot = qidx % 4
                if qidx >= 4:
                    b.wait("gpsimd", "st", w2_use_ms[qidx - 4])
                w2_tgts[qidx] = b.dma("gpsimd", ARENA[:, slot * 2048:slot * 2048 + 2048],
                                      w2g[g_, m_], "wg%d" % slot)

            for _u in range(NTT):
                sel_mm(0, _u)
                sel_dve(0, _u)
            it = 0
            for s_ in range(NSLAB):
                off = (s_ % 2) * (CH * 512)
                b.wait("tensor", "w1%d" % (s_ % 2), slab_tgts[s_])
                for sub in range(4):
                    du = None
                    if it % 2 == 0:
                        sg_ = s_ // 4 + 1
                        if sg_ < NSTRIP:
                            du = (sg_, ((s_ % 4) * 4 + sub) // 2)
                            sel_mm(*du)
                    if it >= 2:
                        b.wait("tensor", "sv", it_sel[it - 2])
                    ms = None
                    for tt2 in range(2):
                        tt = sub * 2 + tt2
                        for k in range(CH):
                            ms = b.ins("tensor", lambda e, tt=tt, k=k, off=off, q=(it % 2) * 2 + tt2: e.matmul(
                                PS[:, q * 512:q * 512 + 512],
                                xln[:, k * T + tt * 128:k * T + tt * 128 + 128],
                                sl_area[:, off + k * 512:off + k * 512 + 512],
                                start=(k == 0), stop=(k == CH - 1)),
                                inc=("st", 1) if (k == CH - 1 and tt2 == 1) else None)
                    b.wait("vector", "st", ms)
                    b.wait("vector", "sg", sel_ms[(s_ // 4) * NTT + sub * 2 + 1])
                    last = None
                    for tt2 in range(2):
                        tt = sub * 2 + tt2
                        q = (it % 2) * 2 + tt2
                        for hf in range(2):
                            n = s_ * 2 + hf
                            ji = (tt2 * 2 + hf) * 256
                            last = b.ins("vector", lambda e, tt=tt, q=q, hf=hf, n=n, ji=ji: e.scalar_tensor_tensor(
                                junk[:, ji:ji + 256], iota256[:, :],
                                idx_all[:, tt * NB + n:tt * NB + n + 1],
                                PS[:, q * 512 + hf * 256:q * 512 + hf * 256 + 256],
                                AluOpType.is_equal, AluOpType.mult,
                                accum_out=h_all[:, tt * NB + n:tt * NB + n + 1]),
                                inc=("sv", 1))
                    it_sel[it] = last
                    it += 1
                    if du is not None:
                        sel_dve(*du)
                if s_ == 2:
                    for _q in range(4):
                        w2_dma(_q)
                if s_ + 2 < NSLAB:
                    w1_dma(s_ + 2)
            w1_last = b.cnt["st"]
            ps_tr2 = PS[0:64, 2048:2176]
            b.wait("tensor", "sg", sel_ms[NSTRIP * NTT - 1])
            for tt in range(NTT):
                tp = b.ins("tensor", lambda e, tt=tt: e.transpose(
                    ps_tr2[:NB, :], idx_all[:, tt * NB:(tt + 1) * NB], ident[:, :]),
                    inc=("st", 1))
                b.wait("vector", "st", tp)
                ev = b.ins("vector", lambda e, tt=tt: e.tensor_copy(
                    rowstage[:NB, tt * 128:(tt + 1) * 128], ps_tr2[:NB, :]), inc=("sv", 1))
                b.wait("tensor", "sv", ev)
            b.wait("sync", "sv", b.cnt["sv"])
            idx_st = b.dma("sync", s_rows[0:NB, :], rowstage[:NB, :], "sp")
            b.wait("tensor", "sv", b.cnt["sv"])
            b.wait("vector", "sp", idx_st)
            for tt in range(NTT):
                tp = b.ins("tensor", lambda e, tt=tt: e.transpose(
                    ps_tr[:NB, :], h_all[:, tt * NB:(tt + 1) * NB], ident[:, :]),
                    inc=("st", 1))
                b.wait("vector", "st", tp)
                ev = b.ins("vector", lambda e, tt=tt: e.tensor_scalar(
                    rowstage[:NB, tt * 128:(tt + 1) * 128], ps_tr[:NB, :],
                    0.5, 0.0, AluOpType.mult, AluOpType.max), inc=("sv", 1))
                b.wait("tensor", "sv", ev)
            b.wait("sync", "sv", b.cnt["sv"])
            h_st = b.dma("sync", s_rows[NB:2 * NB, :], rowstage[:NB, :], "sp")

            rep_tgts = {}
            ht_ms = {}
            pe_g_ms = {}
            add_ms = {}

            def rep_dma(n):
                dst = rowsA if (n % 2 == 0) else rowsB
                sname = "rw%d" % (n % 2)
                if n >= 2:
                    b.wait("sync", "sv", ht_ms[2 * (n - 2) + 1])
                b.dma("sync", dst[:, 0:T],
                      bass.AP(s_rows.tensor, n * T, [[0, 128], [1, T]]), sname)
                rep_tgts[n] = b.dma("sync", dst[:, T:2 * T],
                                    bass.AP(s_rows.tensor, (NB + n) * T, [[0, 128], [1, T]]),
                                    sname)

            b.wait("sync", "sp", h_st)
            rep_dma(0)
            rep_dma(1)

            def build_h(gb, jc):
                c = gb * JG + jc
                n = c // 2
                rows = rowsA if (n % 2 == 0) else rowsB
                b.wait("vector", "rw%d" % (n % 2), rep_tgts[n])
                if gb > 1:
                    b.wait("vector", "st", pe_g_ms[gb - 2])
                hsl = ((gb % 2) * JG + jc) * T
                hm = b.ins("vector", lambda e, hsl=hsl, rows=rows, c=c, xln=xln: e.scalar_tensor_tensor(
                    xln[:, hsl:hsl + T], rows[:, 0:T],
                    iotac[:, c % 2:c % 2 + 1], rows[:, T:2 * T],
                    AluOpType.is_equal, AluOpType.mult), inc=("sv", 1))
                ht_ms[c] = hm
                if jc % 2 == 1 and n + 2 < NB:
                    rep_dma(n + 2)

            qidx = 0
            for g in range(NGRP):
                if g == 0:
                    b.wait("vector", "st", w1_last)
                    for jc in range(JG):
                        build_h(0, jc)
                for m in range(CH):
                    slot = qidx % 4
                    oslot = m % 2
                    b.wait("tensor", "wg%d" % slot, w2_tgts[qidx])
                    if m > 0:
                        b.wait("tensor", "sv", ht_ms[g * JG + JG - 1])
                    lin = g * CH + m
                    if lin >= 2:
                        b.wait("tensor", "sv", add_ms[lin - 2])
                    ms = None
                    for jc in range(JG):
                        if m == 0:
                            b.wait("tensor", "sv", ht_ms[g * JG + jc])
                        for sg in range(SEG):
                            hsl = ((g % 2) * JG + jc) * T
                            ms = b.ins("tensor", lambda e, hsl=hsl, jc=jc, sg=sg, slot=slot, oslot=oslot, xln=xln: e.matmul(
                                PS[:, oslot * OSL + sg * SEGW:oslot * OSL + sg * SEGW + SEGW],
                                ARENA[:, slot * 2048 + jc * 128:slot * 2048 + jc * 128 + 128],
                                xln[:, hsl + sg * SEGW:hsl + sg * SEGW + SEGW],
                                start=(jc == 0), stop=(jc == JG - 1)),
                                inc=("st", 1) if (jc == JG - 1 and sg == SEG - 1) else None)
                    w2_use_ms[qidx] = ms
                    if qidx + 4 < NQI:
                        w2_dma(qidx + 4)
                    qidx += 1
                    pe_g_ms[g] = ms
                    b.wait("vector", "st", ms)
                    dst = work[:, m * T:(m + 1) * T]
                    if g == 0:
                        am = b.ins("vector", lambda e, dst=dst, oslot=oslot: e.tensor_copy(
                            dst, PS[:, oslot * OSL:oslot * OSL + T]), inc=("sv", 1))
                    elif g < NGRP - 1:
                        am = b.ins("vector", lambda e, dst=dst, oslot=oslot: e.tensor_tensor(
                            dst, dst, PS[:, oslot * OSL:oslot * OSL + T], AluOpType.add),
                            inc=("sv", 1))
                    else:
                        if state["res_in_x"]:
                            rsld = xstage[:, (m % 2) * T:(m % 2) * T + T]
                        else:
                            rsld = tstage[:, (m % 2) * T:(m % 2) * T + T]
                        if m >= 2:
                            b.wait("sync", "sv", add_ms[lin - 2])
                            if final and (m - 2) in out_tgts:
                                b.wait("sync", "ox%d" % (m % 2), out_tgts[m - 2])
                        tld = b.dma("sync", rsld, res_src(m), "rs%d" % (m % 2))
                        b.wait("vector", "rs%d" % (m % 2), tld)
                        a1 = b.ins("vector", lambda e, dst=dst, oslot=oslot: e.tensor_tensor(
                            dst, dst, PS[:, oslot * OSL:oslot * OSL + T], AluOpType.add),
                            inc=("sv", 1))
                        b.wait("vector", "sv", a1)
                        if final:
                            rs = xstage[:, (m % 2) * T:(m % 2) * T + T]
                            a2 = b.ins("vector", lambda e, dst=dst, rs=rs, rsld=rsld: e.tensor_tensor(
                                rs, dst, rsld, AluOpType.add), inc=("sv", 1))
                            b.wait("sync", "sv", a2)
                            out_tgts[m] = b.dma("sync", out_fm[m * 128:(m + 1) * 128, :], rs,
                                                "ox%d" % (m % 2))
                        else:
                            a2 = b.ins("vector", lambda e, dst=dst, rsld=rsld: e.tensor_tensor(
                                dst, dst, rsld, AluOpType.add), inc=("sv", 1))
                            b.wait("sync", "sv", a2)
                            b.dma("sync", s_res[:, m * T:(m + 1) * T], dst, "sp")
                        am = a2
                    add_ms[lin] = am
                    if g + 1 < NGRP and m < JG:
                        build_h(g + 1, m)
            b.barrier()
            state["res_in_x"] = False

        def proj_phase(xln, wdram, bias_idx, dst_dram, evac, xln_ready=None, pre=None):
            slab_tgts = {}
            use_ms = {}
            ev_ms = {}
            oev_ms = {}

            pbufs = (AR3[:, :], AR2[:, :], ARENA[:, 0:4096], ARENA[:, 4096:8192])

            def slab_dma(m):
                if m >= 4:
                    b.wait("gpsimd", "st", use_ms[m - 4])
                slab_tgts[m] = b.dma("gpsimd", pbufs[m % 4], wdram[m], "pj%d" % (m % 4))

            if pre is None:
                for _m in range(4):
                    slab_dma(_m)
            else:
                slab_tgts[0], slab_tgts[1] = pre
                slab_dma(2)
                slab_dma(3)
            for m in range(CH):
                wt = pbufs[m % 4]
                oslot = m % 2
                b.wait("tensor", "pj%d" % (m % 4), slab_tgts[m])
                if m >= 2:
                    b.wait("tensor", "sv", ev_ms[m - 2])
                ms = None
                for k in range(CH):
                    if m == 0 and xln_ready is not None:
                        b.wait("tensor", "sv", xln_ready[k])
                    for sg in range(SEG):
                        rhs = xln[:, k * T + sg * SEGW:k * T + sg * SEGW + SEGW]
                        ms = b.ins("tensor", lambda e, wt=wt, k=k, sg=sg, oslot=oslot, rhs=rhs: e.matmul(
                            PS[:, oslot * OSL + sg * SEGW:oslot * OSL + sg * SEGW + SEGW],
                            wt[:, k * 128:(k + 1) * 128], rhs,
                            start=(k == 0), stop=(k == CH - 1)),
                            inc=("st", 1) if (k == CH - 1 and sg == SEG - 1) else None)
                use_ms[m] = ms
                if m + 4 < CH:
                    slab_dma(m + 4)
                b.wait("vector", "st", ms)
                if evac == "bias_sbuf_q":
                    SQ_ = T // 4
                    dq = dst_dram[:, :].rearrange("p (q c t) -> p q c t", c=CH, t=SQ_)
                    e1 = b.ins("vector", lambda e, d=dq[:, :, m, :], oslot=oslot, m=m: e.tensor_scalar(
                        d, PS[:, oslot * OSL:oslot * OSL + T].rearrange(
                            "p (q t) -> p q t", t=SQ_),
                        biases[:, bias_idx * CH + m:bias_idx * CH + m + 1], None,
                        AluOpType.add), inc=("sv", 1))
                    ev_ms[m] = e1
                elif evac == "bias_dram_q":
                    dstt2 = tstage[:, (m % 2) * T:(m % 2) * T + T]
                    if m >= 2 and (m - 2) in ev_st:
                        b.wait("vector", "pv%d" % (m % 2), ev_st[m - 2])
                    e1 = b.ins("vector", lambda e, d=dstt2, oslot=oslot, m=m: e.tensor_scalar(
                        d, PS[:, oslot * OSL:oslot * OSL + T],
                        biases[:, bias_idx * CH + m:bias_idx * CH + m + 1], None,
                        AluOpType.add), inc=("sv", 1))
                    ev_ms[m] = e1
                    b.wait("sync", "sv", e1)
                    b.wait("scalar", "sv", e1)
                    SQ_ = T // 4
                    last = None
                    for qd in range(4):
                        last = b.dma("sync" if qd < 2 else "scalar",
                                     bass.AP(dst_dram.tensor, qd * CH * SQ_ + m * SQ_,
                                             [[CH * T, 128], [1, SQ_]]),
                                     dstt2[:, qd * SQ_:(qd + 1) * SQ_],
                                     "pv%d" % (m % 2))
                    ev_st[m] = last
                elif evac == "bias_sbuf":
                    e1 = b.ins("vector", lambda e, d=dst_dram[:, m * T:(m + 1) * T],
                               oslot=oslot, m=m: e.tensor_scalar(
                        d, PS[:, oslot * OSL:oslot * OSL + T],
                        biases[:, bias_idx * CH + m:bias_idx * CH + m + 1], None,
                        AluOpType.add), inc=("sv", 1))
                    ev_ms[m] = e1
                elif evac == "bias_dram":
                    dstt2 = tstage[:, (m % 2) * T:(m % 2) * T + T]
                    if m >= 2 and (m - 2) in ev_st:
                        b.wait("vector", "pv%d" % (m % 2), ev_st[m - 2])
                    e1 = b.ins("vector", lambda e, d=dstt2, oslot=oslot, m=m: e.tensor_scalar(
                        d, PS[:, oslot * OSL:oslot * OSL + T],
                        biases[:, bias_idx * CH + m:bias_idx * CH + m + 1], None,
                        AluOpType.add), inc=("sv", 1))
                    ev_ms[m] = e1
                    b.wait("sync", "sv", e1)
                    ev_st[m] = b.dma("sync", dst_dram[:, m * T:(m + 1) * T], dstt2,
                                     "pv%d" % (m % 2))
                else:
                    rs = rowsA[:, (m % 2) * T:(m % 2) * T + T]
                    if m >= 2:
                        b.wait("sync", "sv", oev_ms[m - 2])
                    tld = b.dma("sync", rs, res_src(m), "rs%d" % (m % 2))
                    dstt2 = tstage[:, (m % 2) * T:(m % 2) * T + T]
                    e1 = b.ins("vector", lambda e, d=dstt2, oslot=oslot, m=m: e.tensor_scalar(
                        d, PS[:, oslot * OSL:oslot * OSL + T],
                        biases[:, 3 * CH + m:3 * CH + m + 1], None,
                        AluOpType.add), inc=("sv", 1))
                    ev_ms[m] = e1
                    b.wait("vector", "rs%d" % (m % 2), tld)
                    b.wait("vector", "sv", e1)
                    e2 = b.ins("vector", lambda e, d=dst_dram[:, m * T:(m + 1) * T], s=dstt2,
                               rs=rs: e.tensor_tensor(d, s, rs, AluOpType.add), inc=("sv", 1))
                    oev_ms[m] = e2
                    b.wait("sync", "sv", e2)
                    b.dma("sync", s_res[:, m * T:(m + 1) * T], dst_dram[:, m * T:(m + 1) * T],
                          "sp")
            ev_st.clear()
            b.barrier()

        def attn_phase(xln, other, xln_ready=None):
            proj_phase(xln, wq, 0, other, "bias_sbuf_q", xln_ready=xln_ready)
            proj_phase(xln, wk, 1, s_k, "bias_dram_q")
            proj_phase(xln, wv, 2, s_v, "bias_dram_q")
            NQ = 4
            SQ = T // 4
            NBAT = SQ // 16
            QW = CH * SQ
            tkq = {}
            for Q in range(NQ):
                tkq[Q] = b.dma("sync" if Q % 2 == 0 else "scalar",
                               xln[:, Q * QW:(Q + 1) * QW],
                               bass.AP(s_k.tensor, Q * QW, [[CH * T, 128], [1, QW]]),
                               "kx%d" % (Q % 2))
            qv4 = other[:, :].rearrange("p (q c t) -> p q c t", c=CH, t=SQ)
            kv4 = xln[:, :].rearrange("p (q c t) -> p q c t", c=CH, t=SQ)

            def rq_issue(Q):
                b.wait("gpsimd", "tb%d" % Q, st_tgt[(Q, NSEGT - 1)])
                last = None
                for grp in range(4):
                    if Q % 2 == 1:
                        last = b.dma("gpsimd", ARENA[grp * 32:grp * 32 + 32, 0:SQ * 32],
                                     s_at[Q * 32:Q * 32 + 32, :], "rq")
                    else:
                        b.dma("gpsimd", AR2[grp * 32:grp * 32 + 32, 0:4096],
                              s_at[Q * 32:Q * 32 + 32, 0:4096], "rq")
                        last = b.dma("gpsimd", AR3[grp * 32:grp * 32 + 32, 0:4096],
                                     s_at[Q * 32:Q * 32 + 32, 4096:8192], "rq")
                return last

            def av_rhs(Q, dl, s_slot):
                if Q % 2 == 1:
                    return ARENA[dl * 32:dl * 32 + 32, s_slot * 32:s_slot * 32 + 32]
                if s_slot < 128:
                    return AR2[dl * 32:dl * 32 + 32, s_slot * 32:s_slot * 32 + 32]
                return AR3[dl * 32:dl * 32 + 32, (s_slot - 128) * 32:(s_slot - 128) * 32 + 32]

            exp_ms = {}
            scr_last = {}
            tvq = {}
            tp_ms = {}
            st_tgt = {}
            TSEG = 1024
            NSEGT = (SQ * 32) // TSEG
            for Q in range(NQ):
                b.wait("tensor", "kx%d" % (Q % 2), tkq[Q])
                for bat in range(NBAT):
                    gi = Q * NBAT + bat
                    pp = gi % 4
                    if gi >= 4:
                        b.wait("tensor", "sv", exp_ms[gi - 4])
                    ms = None
                    for ti in range(16):
                        t_ = bat * 16 + ti
                        ms = b.ins("tensor", lambda e, t_=t_, Q=Q, pp=pp, ti=ti: e.matmul(
                            PS[Q * 32:Q * 32 + 32,
                               pp * 512 + ti * 32:pp * 512 + ti * 32 + 32],
                            qv4[:, Q, :, t_], kv4[:, Q, :, t_], start=True, stop=True,
                            tile_position=(0, Q * 32)),
                            inc=("st", 1) if ti == 15 else None)
                    if bat == NBAT - 1:
                        scr_last[Q] = ms
                    b.wait("scalar", "st", ms)
                    dstA = ARENA[Q * 32:Q * 32 + 32, bat * 512:bat * 512 + 512]
                    srcS = PS[Q * 32:Q * 32 + 32, pp * 512:pp * 512 + 512]
                    a1 = b.ins("scalar", lambda e, dstA=dstA, srcS=srcS: e.activation(
                        out=dstA, in_=srcS, func=AF.Exp, scale=INV_SQRT_D), inc=("sa", 1))
                    b.wait("vector", "sa", a1)
                    v1 = b.ins("vector", lambda e, dstA=dstA, srcS=srcS: e.scalar_tensor_tensor(
                        dstA, srcS, THRESH / INV_SQRT_D, dstA,
                        AluOpType.is_gt, AluOpType.mult), inc=("sv", 1))
                    b.wait("vector", "sv", v1)
                    rt_b = rtile[Q * 32:Q * 32 + 32, bat * 16:bat * 16 + 16]
                    s3 = dstA.rearrange("p (s g) -> p s g", g=32)
                    v2 = b.ins("vector", lambda e, s3=s3, rt_b=rt_b: e.tensor_reduce(
                        rt_b, s3, AX.X, AluOpType.add), inc=("sv", 1))
                    b.wait("vector", "sv", v2)
                    v3 = b.ins("vector", lambda e, rt_b=rt_b: e.reciprocal(rt_b, rt_b),
                               inc=("sv", 1))
                    b.wait("vector", "sv", v3)
                    rbc = rt_b.unsqueeze(2).broadcast_to([32, 16, 32])
                    v4 = b.ins("vector", lambda e, s3=s3, rbc=rbc: e.tensor_tensor(
                        s3, s3, rbc, AluOpType.mult), inc=("sv", 1))
                    exp_ms[gi] = v4
                    if bat % 2 == 1:
                        seg = bat // 2
                        slot = seg % 2
                        stg = tstage[Q * 32:Q * 32 + 32, slot * TSEG:slot * TSEG + TSEG]
                        if seg >= 2:
                            b.wait("vector", "tb%d" % Q, st_tgt[(Q, seg - 2)])
                        sl = slice(seg * TSEG, seg * TSEG + TSEG)
                        tv = b.ins("vector", lambda e, Q=Q, sl=sl, stg=stg: e.transpose(
                            stg, ARENA[Q * 32:Q * 32 + 32, sl]), inc=("sv", 1))
                        tp_ms[(Q, seg)] = tv
                        b.wait("sync", "sv", tv)
                        st_tgt[(Q, seg)] = b.dma(
                            "sync", s_at[Q * 32:Q * 32 + 32, sl], stg, "tb%d" % Q)
                if Q == 0:
                    rq_t = {0: rq_issue(0)}
                b.wait("sync" if Q % 2 == 0 else "scalar", "st", scr_last[Q])
                tvq[Q] = b.dma("sync" if Q % 2 == 0 else "scalar",
                               other[:, Q * QW:(Q + 1) * QW],
                               bass.AP(s_v.tensor, Q * QW, [[CH * T, 128], [1, QW]]),
                               "vx%d" % (Q % 2))
            b.wait("gpsimd", "sv", b.cnt["sv"])
            rq_t[1] = rq_issue(1)
            vv4 = other[:, :].rearrange("p (q c t) -> p q c t", c=CH, t=SQ)
            av_ev = {}
            av_last_mm = {}
            rr_tgts = {}
            wo_pre = {}
            for Q in range(NQ):
                b.wait("tensor", "rq", rq_t[Q])
                b.wait("tensor", "vx%d" % (Q % 2), tvq[Q])
                if Q > 0:
                    b.wait("tensor", "sv", av_ev[Q * NBAT - 1])
                for bat in range(NBAT):
                    gi = Q * NBAT + bat
                    pp = gi % 4
                    if gi >= 4:
                        b.wait("tensor", "sv", av_ev[gi - 4])
                    if Q == 3 and bat < 4:
                        if bat == 0:
                            b.wait("gpsimd", "st", av_last_mm[2])
                        hsl_ = (bat % 2) * 2048
                        wo_buf = AR3 if bat < 2 else AR2
                        wo_pre[bat] = b.dma("gpsimd", wo_buf[:, hsl_:hsl_ + 2048],
                                            wo[bat // 2][:, hsl_:hsl_ + 2048],
                                            "pj%d" % (bat // 2))
                    ms = None
                    for ti in range(16):
                        t_ = bat * 16 + ti
                        s_slot = bat * 16 + ti
                        for dl in range(4):
                            ms = b.ins("tensor", lambda e, t_=t_, dl=dl, pp=pp, ti=ti, s_slot=s_slot, Q=Q: e.matmul(
                                PS[dl * 32:dl * 32 + 32,
                                   2048 + pp * 512 + ti * 32:2048 + pp * 512 + ti * 32 + 32],
                                vv4[dl * 32:dl * 32 + 32, Q, :, t_],
                                av_rhs(Q, dl, s_slot),
                                start=True, stop=True, tile_position=(dl * 32, dl * 32)),
                                inc=("st", 1) if (ti == 15 and dl == 3) else None)
                    av_last_mm[Q] = ms
                    b.wait("vector", "st", ms)
                    dst3 = xln[:, :].rearrange("p (h t) -> p h t", t=T)[
                        :, :, Q * SQ + bat * 16:Q * SQ + bat * 16 + 16]
                    ev = b.ins("vector", lambda e, dst3=dst3, pp=pp: e.tensor_copy(
                        dst3,
                        PS[:, 2048 + pp * 512:2048 + pp * 512 + 512].rearrange(
                            "p (t h) -> p h t", h=32)), inc=("sv", 1))
                    av_ev[gi] = ev
                    if bat == NBAT - 1 and Q + 2 < NQ:
                        b.wait("gpsimd", "st", av_last_mm[Q])
                        rq_t[Q + 2] = rq_issue(Q + 2)
            b.barrier()
            proj_phase(xln, wo, 3, other, "o_evac", pre=(wo_pre[1], wo_pre[3]))

        vms0 = ln_phase(None, TA, True)
        if "ffn1" in phases:
            ffn_phase(TA, TB, final=False, xln_ready=vms0)
            vms1 = ln_phase(TB, TB, False)
        if "attn" in phases:
            attn_phase(TB, TA, xln_ready=vms1)
            vms2 = ln_phase(TA, TA, False)
        if "ffn2" in phases:
            ffn_phase(TA, TB, final=True, xln_ready=vms2)
        for s in list(b.sems):
            if s.startswith("ox") or s == "sp":
                b.wait("sync", s, b.cnt[s])

        @blk.sync
        def _(e):
            for f in b.ops["sync"]:
                f(e)

        @blk.scalar
        def _(e):
            for f in b.ops["scalar"]:
                f(e)

        @blk.vector
        def _(e):
            for f in b.ops["vector"]:
                f(e)

        @blk.tensor
        def _(e):
            for f in b.ops["tensor"]:
                f(e)

        @blk.gpsimd
        def _(e):
            for f in b.ops["gpsimd"]:
                f(e)

    return nc



def _tile_w(wT):
    w = wT.reshape(CH, 128, CH, 128)
    return np.ascontiguousarray(w.transpose(2, 1, 0, 3)).reshape(CH, 128, CH * 128)


def prepare_inputs(inp):
    g = {}
    Wq = np.asarray(inp["L0_attn_Wq"]); Wk = np.asarray(inp["L0_attn_Wk"])
    Wv = np.asarray(inp["L0_attn_Wv"]); Wo = np.asarray(inp["L0_attn_Wo"])
    g["wq"] = _tile_w(np.ascontiguousarray(Wq.T))
    g["wk"] = _tile_w(np.ascontiguousarray(Wk.T))
    perm_v = np.zeros(D, dtype=np.int64)
    for c in range(CH):
        for dl in range(4):
            for gg in range(32):
                perm_v[c * 128 + dl * 32 + gg] = gg * 128 + 4 * c + dl
    g["wv"] = _tile_w(np.ascontiguousarray(np.ascontiguousarray(Wv.T)[:, perm_v]))
    perm_o = np.zeros(D, dtype=np.int64)
    for h in range(H):
        for p in range(128):
            dl, dh = p // 32, p % 32
            perm_o[h * 128 + p] = h * 128 + 4 * dh + dl
    WoTp = np.ascontiguousarray(Wo.T)[perm_o, :]
    w = WoTp.reshape(CH, 128, CH, 128)
    g["wo"] = np.ascontiguousarray(w.transpose(2, 1, 0, 3)).reshape(CH, 128, CH * 128)
    g["bq"] = np.ascontiguousarray(np.asarray(inp["L0_attn_bq"]).reshape(CH, 128).T)
    g["bk"] = np.ascontiguousarray(np.asarray(inp["L0_attn_bk"]).reshape(CH, 128).T)
    g["bv"] = np.ascontiguousarray(np.asarray(inp["L0_attn_bv"])[perm_v].reshape(CH, 128).T)
    g["bo"] = np.ascontiguousarray(np.asarray(inp["L0_attn_bo"]).reshape(CH, 128).T)
    w1T = np.ascontiguousarray(np.asarray(inp["L0_ffn_w1"]).T)
    t = w1T.reshape(CH, 128, 28, 512)
    g["w1t"] = np.ascontiguousarray(t.transpose(2, 1, 0, 3)).reshape(28, 128, CH * 512)
    w2 = np.asarray(inp["L0_ffn_w2t"])
    t2 = w2.reshape(NGRP, JG, 128, CH, 128)
    g["w2g"] = np.ascontiguousarray(t2.transpose(0, 3, 2, 1, 4)).reshape(
        NGRP, CH, 128, JG * 128)
    c1T = np.ascontiguousarray(np.asarray(inp["L0_ffn_c1_w"]).T).reshape(CH, 128, RANK)
    g["c1t"] = np.ascontiguousarray(c1T.transpose(1, 0, 2)).reshape(128, CH * RANK)
    g["c2t"] = np.ascontiguousarray(np.asarray(inp["L0_ffn_c2_w"]).T)
    return g


def kernel(**inputs):
    x = np.asarray(inputs["L0_attn_x"])
    Bx, S, Dm = x.shape
    xt = x.reshape(Bx * S, Dm)
    g = prepare_inputs(inputs)
    T = (Bx * S) // NCORES
    nc = build_program(T=T)
    in_maps = []
    for c in range(NCORES):
        m = dict(g)
        m["x"] = np.ascontiguousarray(xt[c * T:(c + 1) * T, :].T)
        in_maps.append(m)
    res = run_bass_kernel_spmd(nc, in_maps, core_ids=list(range(NCORES)))
    outs = [np.asarray(res.results[c]["out"]).T for c in range(NCORES)]
    return np.concatenate(outs, axis=0).reshape(Bx, S, Dm).astype(np.float32)


# version 11
# speedup vs baseline: 1.0417x; 1.0417x over previous
import numpy as np
import concourse.bass as bass
import concourse.mybir as mybir
from concourse.bass_utils import run_bass_kernel_spmd
from concourse.alu_op_type import AluOpType

F32 = mybir.dt.float32
BF16 = mybir.dt.bfloat16
U32 = mybir.dt.uint32
AF = mybir.ActivationFunctionType
AX = mybir.AxisListType

NCORES = 8
D = 4096
CH = 32
H = 32
HD = 128
NJ = 14336
NB = 56
RANK = 16
EPS = 1e-5
INV_SQRT_D = float(np.float32(1.0) / np.float32(np.sqrt(np.float32(128.0))))
THRESH = 0.001
JG = 16
NGRP = NJ // 128 // JG

ENGINES = ("sync", "scalar", "vector", "tensor", "gpsimd")


class B:
    def __init__(self, nc):
        self.nc = nc
        self.ops = {e: [] for e in ENGINES}
        self.sems = {}
        self.cnt = {}
        self.ctxs = []

    def sem(self, name):
        if name in self.sems:
            return
        h = self.nc.semaphore(name)
        self.sems[name] = h.__enter__()
        self.ctxs.append(h)
        self.cnt[name] = 0

    def emit(self, eng, f):
        self.ops[eng].append(f)

    def ins(self, eng, build_fn, inc=None):
        tgt = None
        if inc is not None:
            s, k = inc
            self.sem(s)
            self.cnt[s] += k
            tgt = self.cnt[s]

        def f(e, build_fn=build_fn, inc=inc):
            h = build_fn(e)
            if inc is not None:
                h.then_inc(self.sems[inc[0]], inc[1])
        self.emit(eng, f)
        return tgt

    def wait(self, eng, sem, tgt):
        if tgt is None or tgt <= 0:
            return
        self.sem(sem)
        self.emit(eng, lambda e, s=sem, t=tgt: e.wait_ge(self.sems[s], t))

    def dma(self, queue, out, in_, sem):
        self.sem(sem)
        return self.ins(queue, lambda e, o=out, i=in_: e.dma_start(out=o, in_=i),
                        inc=(sem, 16))

    def barrier(self, skip=()):
        snap = {s: self.cnt[s] for s in self.sems}
        for eng in ENGINES:
            if eng in skip:
                continue
            for s, t in snap.items():
                if t > 0:
                    self.wait(eng, s, t)


def build_program(T=1024, phases=("ffn1", "attn", "ffn2")):
    NTT = T // 128
    SEG = max(1, T // 512)
    SEGW = min(512, T)
    nc = bass.Bass("TRN2", target_bir_lowering=False)

    def din(name, shape, dtype=F32):
        return nc.dram_tensor(name, shape, dtype, kind="ExternalInput").ap()

    x_fm = din("x", [D, T])
    wq = din("wq", [CH, 128, CH * 128])
    wk = din("wk", [CH, 128, CH * 128])
    wv = din("wv", [CH, 128, CH * 128])
    wo = din("wo", [CH, 128, CH * 128])
    bq = din("bq", [128, CH])
    bk = din("bk", [128, CH])
    bv = din("bv", [128, CH])
    bo = din("bo", [128, CH])
    w1t = din("w1t", [28, 128, CH * 512])
    w2g = din("w2g", [NGRP, CH, 128, JG * 128])
    c1t = din("c1t", [128, CH * RANK])
    c2t = din("c2t", [RANK, NJ])
    out_fm = nc.dram_tensor("out", [D, T], F32, kind="ExternalOutput").ap()

    s_res = nc.dram_tensor("s_res", [128, CH * T], BF16).ap()
    s_q = nc.dram_tensor("s_q", [128, CH * T], BF16).ap()
    s_k = nc.dram_tensor("s_k", [128, CH * T], BF16).ap()
    s_v = nc.dram_tensor("s_v", [128, CH * T], BF16).ap()
    s_at = nc.dram_tensor("s_at", [128, (T // 4) * H], BF16).ap()
    s_rows = nc.dram_tensor("s_rows", [2 * NB, T], BF16).ap()
    s_ln = nc.dram_tensor("s_ln", [2, T], F32).ap()
    s_r = nc.dram_tensor("s_r", [128, T // 4], F32).ap()

    import contextlib
    with contextlib.ExitStack() as ctx:
        def sb(name, shape, dtp):
            return ctx.enter_context(nc.sbuf_tensor(name, shape, dtp))

        TA = sb("TA", [128, CH * T], BF16)
        TB = sb("TB", [128, CH * T], BF16)
        ARENA = sb("ARENA", [128, 8192], BF16)
        AR2 = sb("AR2", [128, 4096], BF16)
        AR3 = sb("AR3", [128, 4096], BF16)
        xstage = sb("xstage", [128, 2 * T], F32)
        tstage = sb("tstage", [128, 2 * T], BF16)
        repm = sb("repm", [128, T], BF16)
        repr_ = sb("repr", [128, T], BF16)
        rrep = sb("rrep", [128, 2 * 512], F32)
        rowsA = sb("rowsA", [128, 2 * T], BF16)
        rowsB = sb("rowsB", [128, 2 * T], BF16)
        lnbf = sb("lnbf", [1, 2 * T], BF16)
        onesrow = sb("onesrow", [1, 128], BF16)
        c1sb = sb("c1sb", [128, CH * RANK], BF16)
        zsb = sb("zsb", [RANK, T], BF16)
        idx_all = sb("idx_all", [128, NTT * NB], F32)
        h_all = sb("h_all", [128, NTT * NB], F32)
        mx8 = sb("mx8", [128, 8], F32)
        ix8u = sb("ix8u", [128, 8], U32)
        ix8f = sb("ix8f", [128, 8], F32)
        iota8 = sb("iota8", [128, 8], F32)
        iota256 = sb("iota256", [128, 256], F32)
        iotac = sb("iotac", [128, 2], F32)
        ident = sb("ident", [128, 128], F32)
        onesf = sb("onesf", [128, 1], F32)
        onesb = sb("onesb", [128, 1], BF16)
        epscol = sb("epscol", [1, 1], F32)
        biases = sb("biases", [128, 4 * CH], F32)
        rtile = sb("rtile", [128, T // 4], F32)

        rowstage = AR2[0:64, 0:T]
        junk = AR3[:, 0:2048]
        w1sl = sb("w1sl", [128, 2 * CH * 512], BF16) if CH * T < 2 * CH * 512 else None

        PS = ctx.enter_context(nc.psum_tensor("PS", [128, 4096], F32))
        ps_z = PS[0:RANK, 0:T]
        ps_stx = PS[0:1, 2048:2048 + T]
        ps_stq = PS[0:1, 3072:3072 + T]
        OSL = max(T, 512)
        ps_tr = PS[0:64, 0:128]

        b = B(nc)
        out_tgts = {}
        ev_st = {}

        blk = ctx.enter_context(nc.Block())

        b.ins("gpsimd", lambda e: e.iota(iota256[:, :], [[1, 256]], channel_multiplier=0,
                                         allow_small_or_imprecise_dtypes=True))
        b.ins("gpsimd", lambda e: e.iota(iota8[:, :], [[256, 8]], channel_multiplier=0,
                                         allow_small_or_imprecise_dtypes=True))
        b.ins("gpsimd", lambda e: e.iota(iotac[:, 0:1], [[1, 1]], channel_multiplier=1,
                                         allow_small_or_imprecise_dtypes=True))
        b.ins("gpsimd", lambda e: e.iota(iotac[:, 1:2], [[1, 1]], base=128, channel_multiplier=1,
                                         allow_small_or_imprecise_dtypes=True))
        b.ins("gpsimd", lambda e: e.iota(ident[:, :], [[1, 128]], channel_multiplier=0,
                                         allow_small_or_imprecise_dtypes=True))
        b.ins("gpsimd", lambda e: e.memset(onesf[:, :], 1.0))
        b.ins("gpsimd", lambda e: e.memset(onesb[:, :], 1.0))
        b.ins("gpsimd", lambda e: e.memset(onesrow[:, :], 1.0))
        gp0 = b.ins("gpsimd", lambda e: e.memset(epscol[:, :], EPS), inc=("sg", 1))
        b.dma("gpsimd", c1sb[:, :], c1t, "sw")
        b.dma("sync", biases[:, 0 * CH:1 * CH], bq, "sx")
        b.dma("sync", biases[:, 1 * CH:2 * CH], bk, "sx")
        b.dma("sync", biases[:, 2 * CH:3 * CH], bv, "sx")
        b.dma("sync", biases[:, 3 * CH:4 * CH], bo, "sx")
        b.wait("vector", "sg", gp0)
        b.ins("vector", lambda e: e.tensor_scalar(ident[:, :], ident[:, :], iotac[:, 0:1],
                                                  None, AluOpType.is_equal), inc=("sv", 1))
        b.barrier()

        state = {"res_in_x": True}

        def res_src(m):
            if state["res_in_x"]:
                return x_fm[m * 128:(m + 1) * 128, :]
            return s_res[:, m * T:(m + 1) * T]

        def ln_phase(src_tile, dst_tile, src_is_x):
            ld1 = {}
            cons_ms = {}
            cvt = {}
            if src_is_x:
                for c in range(CH):
                    if c % 2 == 0:
                        ld1[c] = b.dma("gpsimd", dst_tile[:, c * T:(c + 1) * T],
                                       x_fm[c * 128:(c + 1) * 128, :], "lx0")
                    else:
                        xsl = xstage[:, ((c // 2) % 2) * T:((c // 2) % 2) * T + T]
                        if c >= 4:
                            b.wait("sync", "sv", cvt[c - 4])
                        xt_ = b.dma("sync", xsl, x_fm[c * 128:(c + 1) * 128, :], "lx1")
                        b.wait("vector", "lx1", xt_)
                        cvt[c] = b.ins("vector", lambda e, d=dst_tile[:, c * T:(c + 1) * T],
                                       s=xsl: e.tensor_copy(d, s), inc=("sv", 1))
                src_tile = dst_tile
            for c in range(CH):
                if src_is_x:
                    if c % 2 == 0:
                        b.wait("scalar", "lx0", ld1[c])
                    else:
                        b.wait("scalar", "sv", cvt[c])
                src_ap = src_tile[:, c * T:(c + 1) * T]
                sq_ap = tstage[:, (c % 2) * T:(c % 2) * T + T]
                if c >= 2:
                    b.wait("scalar", "st", cons_ms[c - 2])
                ac = b.ins("scalar", lambda e, s=src_ap, d=sq_ap: e.activation(
                    out=d, in_=s, func=AF.Square), inc=("sa", 1))
                b.wait("tensor", "sa", ac)
                last = None
                for g in range(SEG):
                    sl = slice(g * SEGW, g * SEGW + SEGW)
                    b.ins("tensor", lambda e, s=src_ap, sl=sl, c=c: e.matmul(
                        ps_stx[0:1, sl], onesb[:, :], s[:, sl],
                        start=(c == 0), stop=(c == CH - 1)))
                    last = b.ins("tensor", lambda e, s=sq_ap, sl=sl, c=c, g=g: e.matmul(
                        ps_stq[0:1, sl], onesb[:, :], s[:, sl],
                        start=(c == 0), stop=(c == CH - 1)),
                        inc=("st", 1))
                cons_ms[c] = last
            stats_ms = cons_ms[CH - 1]
            M_ = lnbf[0:1, 0:T]; RS = lnbf[0:1, T:2 * T]
            V_ = xstage[0:1, 0:T]; T2_ = xstage[0:1, T:2 * T]
            b.wait("vector", "st", stats_ms)
            v1 = b.ins("vector", lambda e: e.tensor_scalar(
                M_, ps_stx[0:1, :], 1.0 / D, None, AluOpType.mult), inc=("sv", 1))
            b.wait("vector", "sv", v1)
            v2 = b.ins("vector", lambda e: e.tensor_tensor(T2_, M_, M_, AluOpType.mult),
                       inc=("sv", 1))
            v3 = b.ins("vector", lambda e: e.tensor_scalar(
                V_, ps_stq[0:1, :], 1.0 / D, None, AluOpType.mult), inc=("sv", 1))
            b.wait("vector", "sv", v3)
            v4 = b.ins("vector", lambda e: e.tensor_tensor(V_, V_, T2_, AluOpType.subtract),
                       inc=("sv", 1))
            b.wait("scalar", "sv", v4)
            a1 = b.ins("scalar", lambda e: e.activation(out=T2_, in_=V_, func=AF.Sqrt,
                                                        bias=epscol[0:1, 0:1]), inc=("sa", 1))
            b.wait("vector", "sa", a1)

            def _recip_bf(e):
                with nc.allow_low_precision(reason="rstd broadcast row kept in bf16"):
                    return e.reciprocal(RS, T2_)
            v5 = b.ins("vector", _recip_bf, inc=("sv", 1))
            b.wait("tensor", "sv", v1)
            b.ins("tensor", lambda e: e.matmul(
                PS[:, 0:512], onesrow[0:1, :], M_[0:1, 0:512], start=True, stop=True))
            bm1 = b.ins("tensor", lambda e: e.matmul(
                PS[:, 512:1024], onesrow[0:1, :], M_[0:1, 512:1024], start=True, stop=True),
                inc=("st", 1))
            b.wait("tensor", "sv", v5)
            b.ins("tensor", lambda e: e.matmul(
                PS[:, 1024:1536], onesrow[0:1, :], RS[0:1, 0:512], start=True, stop=True))
            bm2 = b.ins("tensor", lambda e: e.matmul(
                PS[:, 1536:2048], onesrow[0:1, :], RS[0:1, 512:1024], start=True, stop=True),
                inc=("st", 1))
            b.wait("vector", "st", bm1)
            e_m = b.ins("vector", lambda e: e.tensor_copy(repm[:, :], PS[:, 0:T]),
                        inc=("sv", 1))
            b.wait("vector", "st", bm2)
            e_r = b.ins("vector", lambda e: e.tensor_copy(repr_[:, :], PS[:, 1024:1024 + T]),
                        inc=("sv", 1))
            vms = {}
            w1s = {}
            slots = (tstage[:, 0:T], tstage[:, T:2 * T], junk[:, 0:T], junk[:, T:2 * T])
            b.wait("vector", "sv", e_r)
            for c in range(CH + 1):
                if c < CH:
                    src_ap = src_tile[:, c * T:(c + 1) * T]
                    tmp = slots[c % 4]
                    if c >= 4:
                        b.wait("vector", "sv", vms[c - 4])
                    w1s[c] = b.ins("vector", lambda e, s=src_ap, d=tmp: e.tensor_tensor(
                        d, s, repm[:, :], AluOpType.subtract), inc=("sv", 1))
                if c >= 1:
                    cc = c - 1
                    b.wait("vector", "sv", w1s[cc])
                    vms[cc] = b.ins("vector", lambda e, d=dst_tile[:, cc * T:(cc + 1) * T],
                                    s=slots[cc % 4]: e.tensor_tensor(
                        d, s, repr_[:, :], AluOpType.mult), inc=("sv", 1))
            b.barrier(skip=("tensor",))
            return vms

        def ffn_phase(xln, work, final, xln_ready=None):
            NSLAB = 28
            sl_area = work if w1sl is None else w1sl[:, :]
            slab_tgts = {}
            it_sel = {}
            NTG = (NTT + 3) // 4

            def w1_dma(s_):
                off = (s_ % 2) * (CH * 512)
                if s_ >= 2:
                    b.wait("gpsimd", "sv", it_sel[(s_ - 2) * 4 + 3])
                slab_tgts[s_] = b.dma("gpsimd", sl_area[:, off:off + CH * 512], w1t[s_],
                                      "w1%d" % (s_ % 2))

            zlast = None
            for k in range(CH):
                if xln_ready is not None:
                    b.wait("tensor", "sv", xln_ready[k])
                for g in range(SEG):
                    sl = slice(g * SEGW, g * SEGW + SEGW)
                    zlast = b.ins("tensor", lambda e, k=k, sl=sl: e.matmul(
                        ps_z[:, sl], c1sb[:, k * RANK:(k + 1) * RANK],
                        xln[:, k * T:(k + 1) * T][:, sl],
                        start=(k == 0), stop=(k == CH - 1)),
                        inc=("st", 1) if (k == CH - 1 and g == SEG - 1) else None)
            b.wait("vector", "st", zlast)
            zev = b.ins("vector", lambda e: e.tensor_copy(zsb[:, :], ps_z[:, :]), inc=("sv", 1))
            b.wait("tensor", "sv", zev)
            NSTRIP = NJ // 2048
            c2_tgts = {}
            strip_done = {}
            c2buf = (AR2[0:RANK, 0:2048], AR2[0:RANK, 2048:4096],
                     AR3[0:RANK, 2048:4096], rowsA[0:RANK, 0:2048])

            def c2_dma(s_):
                if s_ >= 4:
                    b.wait("gpsimd", "st", strip_done[s_ - 4])
                c2_tgts[s_] = b.dma("gpsimd", c2buf[s_ % 4],
                                    c2t[:, s_ * 2048:(s_ + 1) * 2048], "c2%d" % (s_ % 4))

            for _s in range(4):
                c2_dma(_s)
            w1_dma(0)
            w1_dma(1)
            sel_ms = {}
            m2_t = {}
            m3_t = {}
            mm_t = {}

            def sel_mm(s_, tt):
                key = s_ * NTT + tt
                if key == 0:
                    b.wait("tensor", "sv", zev)
                b.wait("tensor", "c2%d" % (s_ % 4), c2_tgts[s_])
                if key >= 1:
                    b.wait("tensor", "sv", m2_t[key - 1])
                ms = None
                for q4 in range(4):
                    ms = b.ins("tensor", lambda e, tt=tt, s_=s_, q4=q4: e.matmul(
                        PS[:, 2048 + q4 * 512:2048 + q4 * 512 + 512],
                        zsb[:, tt * 128:(tt + 1) * 128],
                        c2buf[s_ % 4][:, q4 * 512:q4 * 512 + 512],
                        start=True, stop=True),
                        inc=("st", 1) if q4 == 3 else None)
                if tt == NTT - 1:
                    strip_done[s_] = ms
                    if s_ + 4 < NSTRIP:
                        c2_dma(s_ + 4)
                mm_t[key] = ms

            def sel_dve(s_, tt):
                key = s_ * NTT + tt
                b.wait("vector", "st", mm_t[key])
                s3 = PS[:, 2048:4096].rearrange("p (w j) -> p w j", j=256)
                m1 = b.ins("vector", lambda e, s3=s3: e.reduce_max(mx8[:, :], s3, axis=AX.X),
                           inc=("sv", 1))
                b.wait("vector", "sv", m1)
                m2 = b.ins("vector", lambda e: e.max_index(
                    ix8u[:, :], mx8[:, :], PS[:, 2048:4096]), inc=("sv", 1))
                m2_t[key] = m2
                b.wait("vector", "sv", m2)
                if key >= 1:
                    b.wait("vector", "sg", sel_ms[key - 1])
                m3 = b.ins("vector", lambda e: e.tensor_copy(ix8f[:, :], ix8u[:, :]),
                           inc=("sv", 1))
                m3_t[key] = m3
                b.wait("gpsimd", "sv", m3)
                dst = idx_all[:, tt * NB + s_ * 8:tt * NB + s_ * 8 + 8]
                m4 = b.ins("gpsimd", lambda e, dst=dst: e.tensor_tensor(
                    dst, ix8f[:, :], iota8[:, :], AluOpType.subtract), inc=("sg", 1))
                sel_ms[key] = m4

            w2_tgts = {}
            w2_use_ms = {}
            NQI = NGRP * CH

            def w2_dma(qidx):
                g_, m_ = qidx // CH, qidx % CH
                slot = qidx % 4
                if qidx >= 4:
                    b.wait("gpsimd", "st", w2_use_ms[qidx - 4])
                w2_tgts[qidx] = b.dma("gpsimd", ARENA[:, slot * 2048:slot * 2048 + 2048],
                                      w2g[g_, m_], "wg%d" % slot)

            for _u in range(NTT):
                sel_mm(0, _u)
                sel_dve(0, _u)
            it = 0
            for s_ in range(NSLAB):
                off = (s_ % 2) * (CH * 512)
                b.wait("tensor", "w1%d" % (s_ % 2), slab_tgts[s_])
                for sub in range(4):
                    du = None
                    if it % 2 == 0:
                        sg_ = s_ // 4 + 1
                        if sg_ < NSTRIP:
                            du = (sg_, ((s_ % 4) * 4 + sub) // 2)
                            sel_mm(*du)
                    if it >= 2:
                        b.wait("tensor", "sv", it_sel[it - 2])
                    ms = None
                    for tt2 in range(2):
                        tt = sub * 2 + tt2
                        for k in range(CH):
                            ms = b.ins("tensor", lambda e, tt=tt, k=k, off=off, q=(it % 2) * 2 + tt2: e.matmul(
                                PS[:, q * 512:q * 512 + 512],
                                xln[:, k * T + tt * 128:k * T + tt * 128 + 128],
                                sl_area[:, off + k * 512:off + k * 512 + 512],
                                start=(k == 0), stop=(k == CH - 1)),
                                inc=("st", 1) if (k == CH - 1 and tt2 == 1) else None)
                    b.wait("vector", "st", ms)
                    b.wait("vector", "sg", sel_ms[(s_ // 4) * NTT + sub * 2 + 1])
                    last = None
                    for tt2 in range(2):
                        tt = sub * 2 + tt2
                        q = (it % 2) * 2 + tt2
                        for hf in range(2):
                            n = s_ * 2 + hf
                            ji = (tt2 * 2 + hf) * 256
                            last = b.ins("vector", lambda e, tt=tt, q=q, hf=hf, n=n, ji=ji: e.scalar_tensor_tensor(
                                junk[:, ji:ji + 256], iota256[:, :],
                                idx_all[:, tt * NB + n:tt * NB + n + 1],
                                PS[:, q * 512 + hf * 256:q * 512 + hf * 256 + 256],
                                AluOpType.is_equal, AluOpType.mult,
                                accum_out=h_all[:, tt * NB + n:tt * NB + n + 1]),
                                inc=("sv", 1))
                    it_sel[it] = last
                    it += 1
                    if du is not None:
                        sel_dve(*du)
                if s_ == 2:
                    for _q in range(4):
                        w2_dma(_q)
                if s_ + 2 < NSLAB:
                    w1_dma(s_ + 2)
            w1_last = b.cnt["st"]
            ps_tr2 = PS[0:64, 2048:2176]
            b.wait("tensor", "sg", sel_ms[NSTRIP * NTT - 1])
            for tt in range(NTT):
                tp = b.ins("tensor", lambda e, tt=tt: e.transpose(
                    ps_tr2[:NB, :], idx_all[:, tt * NB:(tt + 1) * NB], ident[:, :]),
                    inc=("st", 1))
                b.wait("vector", "st", tp)
                ev = b.ins("vector", lambda e, tt=tt: e.tensor_copy(
                    rowstage[:NB, tt * 128:(tt + 1) * 128], ps_tr2[:NB, :]), inc=("sv", 1))
                b.wait("tensor", "sv", ev)
            b.wait("sync", "sv", b.cnt["sv"])
            idx_st = b.dma("sync", s_rows[0:NB, :], rowstage[:NB, :], "sp")
            b.wait("tensor", "sv", b.cnt["sv"])
            b.wait("vector", "sp", idx_st)
            for tt in range(NTT):
                tp = b.ins("tensor", lambda e, tt=tt: e.transpose(
                    ps_tr[:NB, :], h_all[:, tt * NB:(tt + 1) * NB], ident[:, :]),
                    inc=("st", 1))
                b.wait("vector", "st", tp)
                ev = b.ins("vector", lambda e, tt=tt: e.tensor_scalar(
                    rowstage[:NB, tt * 128:(tt + 1) * 128], ps_tr[:NB, :],
                    0.5, 0.0, AluOpType.mult, AluOpType.max), inc=("sv", 1))
                b.wait("tensor", "sv", ev)
            b.wait("sync", "sv", b.cnt["sv"])
            h_st = b.dma("sync", s_rows[NB:2 * NB, :], rowstage[:NB, :], "sp")

            rep_tgts = {}
            ht_ms = {}
            pe_g_ms = {}
            add_ms = {}

            def rep_dma(n):
                dst = rowsA if (n % 2 == 0) else rowsB
                sname = "rw%d" % (n % 2)
                if n >= 2:
                    b.wait("sync", "sv", ht_ms[2 * (n - 2) + 1])
                b.dma("sync", dst[:, 0:T],
                      bass.AP(s_rows.tensor, n * T, [[0, 128], [1, T]]), sname)
                rep_tgts[n] = b.dma("sync", dst[:, T:2 * T],
                                    bass.AP(s_rows.tensor, (NB + n) * T, [[0, 128], [1, T]]),
                                    sname)

            b.wait("sync", "sp", h_st)
            rep_dma(0)
            rep_dma(1)

            def build_h(gb, jc):
                c = gb * JG + jc
                n = c // 2
                rows = rowsA if (n % 2 == 0) else rowsB
                b.wait("vector", "rw%d" % (n % 2), rep_tgts[n])
                if gb > 1:
                    b.wait("vector", "st", pe_g_ms[gb - 2])
                hsl = ((gb % 2) * JG + jc) * T
                hm = b.ins("vector", lambda e, hsl=hsl, rows=rows, c=c, xln=xln: e.scalar_tensor_tensor(
                    xln[:, hsl:hsl + T], rows[:, 0:T],
                    iotac[:, c % 2:c % 2 + 1], rows[:, T:2 * T],
                    AluOpType.is_equal, AluOpType.mult), inc=("sv", 1))
                ht_ms[c] = hm
                if jc % 2 == 1 and n + 2 < NB:
                    rep_dma(n + 2)

            qidx = 0
            for g in range(NGRP):
                if g == 0:
                    b.wait("vector", "st", w1_last)
                    for jc in range(JG):
                        build_h(0, jc)
                for m in range(CH):
                    slot = qidx % 4
                    oslot = m % 2
                    b.wait("tensor", "wg%d" % slot, w2_tgts[qidx])
                    if m > 0:
                        b.wait("tensor", "sv", ht_ms[g * JG + JG - 1])
                    lin = g * CH + m
                    if lin >= 2:
                        b.wait("tensor", "sv", add_ms[lin - 2])
                    ms = None
                    for jc in range(JG):
                        if m == 0:
                            b.wait("tensor", "sv", ht_ms[g * JG + jc])
                        for sg in range(SEG):
                            hsl = ((g % 2) * JG + jc) * T
                            ms = b.ins("tensor", lambda e, hsl=hsl, jc=jc, sg=sg, slot=slot, oslot=oslot, xln=xln: e.matmul(
                                PS[:, oslot * OSL + sg * SEGW:oslot * OSL + sg * SEGW + SEGW],
                                ARENA[:, slot * 2048 + jc * 128:slot * 2048 + jc * 128 + 128],
                                xln[:, hsl + sg * SEGW:hsl + sg * SEGW + SEGW],
                                start=(jc == 0), stop=(jc == JG - 1)),
                                inc=("st", 1) if (jc == JG - 1 and sg == SEG - 1) else None)
                    w2_use_ms[qidx] = ms
                    if qidx + 4 < NQI:
                        w2_dma(qidx + 4)
                    qidx += 1
                    pe_g_ms[g] = ms
                    b.wait("vector", "st", ms)
                    dst = work[:, m * T:(m + 1) * T]
                    if g == 0:
                        am = b.ins("vector", lambda e, dst=dst, oslot=oslot: e.tensor_copy(
                            dst, PS[:, oslot * OSL:oslot * OSL + T]), inc=("sv", 1))
                    elif g < NGRP - 1:
                        am = b.ins("vector", lambda e, dst=dst, oslot=oslot: e.tensor_tensor(
                            dst, dst, PS[:, oslot * OSL:oslot * OSL + T], AluOpType.add),
                            inc=("sv", 1))
                    else:
                        if state["res_in_x"]:
                            rsld = xstage[:, (m % 2) * T:(m % 2) * T + T]
                        else:
                            rsld = tstage[:, (m % 2) * T:(m % 2) * T + T]
                        if m >= 2:
                            b.wait("gpsimd", "sv", add_ms[lin - 2])
                            if final and (m - 2) in out_tgts:
                                b.wait("gpsimd", "ox%d" % (m % 2), out_tgts[m - 2])
                        tld = b.dma("gpsimd", rsld, res_src(m), "rs%d" % (m % 2))
                        b.wait("vector", "rs%d" % (m % 2), tld)
                        a1 = b.ins("vector", lambda e, dst=dst, oslot=oslot: e.tensor_tensor(
                            dst, dst, PS[:, oslot * OSL:oslot * OSL + T], AluOpType.add),
                            inc=("sv", 1))
                        b.wait("vector", "sv", a1)
                        if final:
                            rs = xstage[:, (m % 2) * T:(m % 2) * T + T]
                            a2 = b.ins("vector", lambda e, dst=dst, rs=rs, rsld=rsld: e.tensor_tensor(
                                rs, dst, rsld, AluOpType.add), inc=("sv", 1))
                            b.wait("sync", "sv", a2)
                            out_tgts[m] = b.dma("sync", out_fm[m * 128:(m + 1) * 128, :], rs,
                                                "ox%d" % (m % 2))
                        else:
                            a2 = b.ins("vector", lambda e, dst=dst, rsld=rsld: e.tensor_tensor(
                                dst, dst, rsld, AluOpType.add), inc=("sv", 1))
                            b.wait("sync", "sv", a2)
                            b.dma("sync", s_res[:, m * T:(m + 1) * T], dst, "sp")
                        am = a2
                    add_ms[lin] = am
                    if g + 1 < NGRP and m < JG:
                        build_h(g + 1, m)
            b.barrier()
            state["res_in_x"] = False

        def proj_phase(xln, wdram, bias_idx, dst_dram, evac, xln_ready=None, pre=None):
            slab_tgts = {}
            use_ms = {}
            ev_ms = {}
            oev_ms = {}

            pbufs = (AR3[:, :], AR2[:, :], ARENA[:, 0:4096], ARENA[:, 4096:8192])

            def slab_dma(m):
                if m >= 4:
                    b.wait("gpsimd", "st", use_ms[m - 4])
                slab_tgts[m] = b.dma("gpsimd", pbufs[m % 4], wdram[m], "pj%d" % (m % 4))

            if pre is None:
                for _m in range(4):
                    slab_dma(_m)
            else:
                slab_tgts[0], slab_tgts[1] = pre
                slab_dma(2)
                slab_dma(3)
            for m in range(CH):
                wt = pbufs[m % 4]
                oslot = m % 2
                b.wait("tensor", "pj%d" % (m % 4), slab_tgts[m])
                if m >= 2:
                    b.wait("tensor", "sv", ev_ms[m - 2])
                ms = None
                for k in range(CH):
                    if m == 0 and xln_ready is not None:
                        b.wait("tensor", "sv", xln_ready[k])
                    for sg in range(SEG):
                        rhs = xln[:, k * T + sg * SEGW:k * T + sg * SEGW + SEGW]
                        ms = b.ins("tensor", lambda e, wt=wt, k=k, sg=sg, oslot=oslot, rhs=rhs: e.matmul(
                            PS[:, oslot * OSL + sg * SEGW:oslot * OSL + sg * SEGW + SEGW],
                            wt[:, k * 128:(k + 1) * 128], rhs,
                            start=(k == 0), stop=(k == CH - 1)),
                            inc=("st", 1) if (k == CH - 1 and sg == SEG - 1) else None)
                use_ms[m] = ms
                if m + 4 < CH:
                    slab_dma(m + 4)
                b.wait("vector", "st", ms)
                if evac == "bias_sbuf_q":
                    SQ_ = T // 4
                    dq = dst_dram[:, :].rearrange("p (q c t) -> p q c t", c=CH, t=SQ_)
                    e1 = b.ins("vector", lambda e, d=dq[:, :, m, :], oslot=oslot, m=m: e.tensor_scalar(
                        d, PS[:, oslot * OSL:oslot * OSL + T].rearrange(
                            "p (q t) -> p q t", t=SQ_),
                        biases[:, bias_idx * CH + m:bias_idx * CH + m + 1], None,
                        AluOpType.add), inc=("sv", 1))
                    ev_ms[m] = e1
                elif evac == "bias_dram_q":
                    dstt2 = tstage[:, (m % 2) * T:(m % 2) * T + T]
                    if m >= 2 and (m - 2) in ev_st:
                        b.wait("vector", "pv%d" % (m % 2), ev_st[m - 2])
                    e1 = b.ins("vector", lambda e, d=dstt2, oslot=oslot, m=m: e.tensor_scalar(
                        d, PS[:, oslot * OSL:oslot * OSL + T],
                        biases[:, bias_idx * CH + m:bias_idx * CH + m + 1], None,
                        AluOpType.add), inc=("sv", 1))
                    ev_ms[m] = e1
                    b.wait("sync", "sv", e1)
                    b.wait("scalar", "sv", e1)
                    SQ_ = T // 4
                    last = None
                    for qd in range(4):
                        last = b.dma("sync" if qd < 2 else "scalar",
                                     bass.AP(dst_dram.tensor, qd * CH * SQ_ + m * SQ_,
                                             [[CH * T, 128], [1, SQ_]]),
                                     dstt2[:, qd * SQ_:(qd + 1) * SQ_],
                                     "pv%d" % (m % 2))
                    ev_st[m] = last
                elif evac == "bias_sbuf":
                    e1 = b.ins("vector", lambda e, d=dst_dram[:, m * T:(m + 1) * T],
                               oslot=oslot, m=m: e.tensor_scalar(
                        d, PS[:, oslot * OSL:oslot * OSL + T],
                        biases[:, bias_idx * CH + m:bias_idx * CH + m + 1], None,
                        AluOpType.add), inc=("sv", 1))
                    ev_ms[m] = e1
                elif evac == "bias_dram":
                    dstt2 = tstage[:, (m % 2) * T:(m % 2) * T + T]
                    if m >= 2 and (m - 2) in ev_st:
                        b.wait("vector", "pv%d" % (m % 2), ev_st[m - 2])
                    e1 = b.ins("vector", lambda e, d=dstt2, oslot=oslot, m=m: e.tensor_scalar(
                        d, PS[:, oslot * OSL:oslot * OSL + T],
                        biases[:, bias_idx * CH + m:bias_idx * CH + m + 1], None,
                        AluOpType.add), inc=("sv", 1))
                    ev_ms[m] = e1
                    b.wait("sync", "sv", e1)
                    ev_st[m] = b.dma("sync", dst_dram[:, m * T:(m + 1) * T], dstt2,
                                     "pv%d" % (m % 2))
                else:
                    rs = rowsA[:, (m % 2) * T:(m % 2) * T + T]
                    if m >= 2:
                        b.wait("gpsimd", "sv", oev_ms[m - 2])
                    tld = b.dma("gpsimd", rs, res_src(m), "rs%d" % (m % 2))
                    dstt2 = tstage[:, (m % 2) * T:(m % 2) * T + T]
                    e1 = b.ins("vector", lambda e, d=dstt2, oslot=oslot, m=m: e.tensor_scalar(
                        d, PS[:, oslot * OSL:oslot * OSL + T],
                        biases[:, 3 * CH + m:3 * CH + m + 1], None,
                        AluOpType.add), inc=("sv", 1))
                    ev_ms[m] = e1
                    b.wait("vector", "rs%d" % (m % 2), tld)
                    b.wait("vector", "sv", e1)
                    e2 = b.ins("vector", lambda e, d=dst_dram[:, m * T:(m + 1) * T], s=dstt2,
                               rs=rs: e.tensor_tensor(d, s, rs, AluOpType.add), inc=("sv", 1))
                    oev_ms[m] = e2
                    b.wait("sync", "sv", e2)
                    b.dma("sync", s_res[:, m * T:(m + 1) * T], dst_dram[:, m * T:(m + 1) * T],
                          "sp")
            ev_st.clear()
            b.barrier()

        def attn_phase(xln, other, xln_ready=None):
            proj_phase(xln, wq, 0, other, "bias_sbuf_q", xln_ready=xln_ready)
            proj_phase(xln, wk, 1, s_k, "bias_dram_q")
            proj_phase(xln, wv, 2, s_v, "bias_dram_q")
            NQ = 4
            SQ = T // 4
            NBAT = SQ // 16
            QW = CH * SQ
            tkq = {}
            for Q in range(NQ):
                tkq[Q] = b.dma("sync" if Q % 2 == 0 else "scalar",
                               xln[:, Q * QW:(Q + 1) * QW],
                               bass.AP(s_k.tensor, Q * QW, [[CH * T, 128], [1, QW]]),
                               "kx%d" % (Q % 2))
            qv4 = other[:, :].rearrange("p (q c t) -> p q c t", c=CH, t=SQ)
            kv4 = xln[:, :].rearrange("p (q c t) -> p q c t", c=CH, t=SQ)

            def rq_issue(Q):
                b.wait("gpsimd", "tb%d" % Q, st_tgt[(Q, NSEGT - 1)])
                last = None
                for grp in range(4):
                    if Q % 2 == 1:
                        last = b.dma("gpsimd", ARENA[grp * 32:grp * 32 + 32, 0:SQ * 32],
                                     s_at[Q * 32:Q * 32 + 32, :], "rq")
                    else:
                        b.dma("gpsimd", AR2[grp * 32:grp * 32 + 32, 0:4096],
                              s_at[Q * 32:Q * 32 + 32, 0:4096], "rq")
                        last = b.dma("gpsimd", AR3[grp * 32:grp * 32 + 32, 0:4096],
                                     s_at[Q * 32:Q * 32 + 32, 4096:8192], "rq")
                return last

            def av_rhs(Q, dl, s_slot):
                if Q % 2 == 1:
                    return ARENA[dl * 32:dl * 32 + 32, s_slot * 32:s_slot * 32 + 32]
                if s_slot < 128:
                    return AR2[dl * 32:dl * 32 + 32, s_slot * 32:s_slot * 32 + 32]
                return AR3[dl * 32:dl * 32 + 32, (s_slot - 128) * 32:(s_slot - 128) * 32 + 32]

            exp_ms = {}
            scr_last = {}
            tvq = {}
            tp_ms = {}
            st_tgt = {}
            TSEG = 1024
            NSEGT = (SQ * 32) // TSEG
            for Q in range(NQ):
                b.wait("tensor", "kx%d" % (Q % 2), tkq[Q])
                for bat in range(NBAT):
                    gi = Q * NBAT + bat
                    pp = gi % 4
                    if gi >= 4:
                        b.wait("tensor", "sv", exp_ms[gi - 4])
                    ms = None
                    for ti in range(16):
                        t_ = bat * 16 + ti
                        ms = b.ins("tensor", lambda e, t_=t_, Q=Q, pp=pp, ti=ti: e.matmul(
                            PS[Q * 32:Q * 32 + 32,
                               pp * 512 + ti * 32:pp * 512 + ti * 32 + 32],
                            qv4[:, Q, :, t_], kv4[:, Q, :, t_], start=True, stop=True,
                            tile_position=(0, Q * 32)),
                            inc=("st", 1) if ti == 15 else None)
                    if bat == NBAT - 1:
                        scr_last[Q] = ms
                    b.wait("scalar", "st", ms)
                    dstA = ARENA[Q * 32:Q * 32 + 32, bat * 512:bat * 512 + 512]
                    srcS = PS[Q * 32:Q * 32 + 32, pp * 512:pp * 512 + 512]
                    a1 = b.ins("scalar", lambda e, dstA=dstA, srcS=srcS: e.activation(
                        out=dstA, in_=srcS, func=AF.Exp, scale=INV_SQRT_D), inc=("sa", 1))
                    b.wait("vector", "sa", a1)
                    v1 = b.ins("vector", lambda e, dstA=dstA, srcS=srcS: e.scalar_tensor_tensor(
                        dstA, srcS, THRESH / INV_SQRT_D, dstA,
                        AluOpType.is_gt, AluOpType.mult), inc=("sv", 1))
                    b.wait("vector", "sv", v1)
                    rt_b = rtile[Q * 32:Q * 32 + 32, bat * 16:bat * 16 + 16]
                    s3 = dstA.rearrange("p (s g) -> p s g", g=32)
                    v2 = b.ins("vector", lambda e, s3=s3, rt_b=rt_b: e.tensor_reduce(
                        rt_b, s3, AX.X, AluOpType.add), inc=("sv", 1))
                    b.wait("vector", "sv", v2)
                    v3 = b.ins("vector", lambda e, rt_b=rt_b: e.reciprocal(rt_b, rt_b),
                               inc=("sv", 1))
                    b.wait("vector", "sv", v3)
                    rbc = rt_b.unsqueeze(2).broadcast_to([32, 16, 32])
                    v4 = b.ins("vector", lambda e, s3=s3, rbc=rbc: e.tensor_tensor(
                        s3, s3, rbc, AluOpType.mult), inc=("sv", 1))
                    exp_ms[gi] = v4
                    if bat % 2 == 1:
                        seg = bat // 2
                        slot = seg % 2
                        stg = tstage[Q * 32:Q * 32 + 32, slot * TSEG:slot * TSEG + TSEG]
                        if seg >= 2:
                            b.wait("vector", "tb%d" % Q, st_tgt[(Q, seg - 2)])
                        sl = slice(seg * TSEG, seg * TSEG + TSEG)
                        tv = b.ins("vector", lambda e, Q=Q, sl=sl, stg=stg: e.transpose(
                            stg, ARENA[Q * 32:Q * 32 + 32, sl]), inc=("sv", 1))
                        tp_ms[(Q, seg)] = tv
                        b.wait("sync", "sv", tv)
                        st_tgt[(Q, seg)] = b.dma(
                            "sync", s_at[Q * 32:Q * 32 + 32, sl], stg, "tb%d" % Q)
                if Q == 0:
                    rq_t = {0: rq_issue(0)}
                b.wait("sync" if Q % 2 == 0 else "scalar", "st", scr_last[Q])
                tvq[Q] = b.dma("sync" if Q % 2 == 0 else "scalar",
                               other[:, Q * QW:(Q + 1) * QW],
                               bass.AP(s_v.tensor, Q * QW, [[CH * T, 128], [1, QW]]),
                               "vx%d" % (Q % 2))
            b.wait("gpsimd", "sv", b.cnt["sv"])
            rq_t[1] = rq_issue(1)
            vv4 = other[:, :].rearrange("p (q c t) -> p q c t", c=CH, t=SQ)
            av_ev = {}
            av_last_mm = {}
            rr_tgts = {}
            wo_pre = {}
            for Q in range(NQ):
                b.wait("tensor", "rq", rq_t[Q])
                b.wait("tensor", "vx%d" % (Q % 2), tvq[Q])
                if Q > 0:
                    b.wait("tensor", "sv", av_ev[Q * NBAT - 1])
                for bat in range(NBAT):
                    gi = Q * NBAT + bat
                    pp = gi % 4
                    if gi >= 4:
                        b.wait("tensor", "sv", av_ev[gi - 4])
                    if Q == 3 and bat < 4:
                        if bat == 0:
                            b.wait("gpsimd", "st", av_last_mm[2])
                        hsl_ = (bat % 2) * 2048
                        wo_buf = AR3 if bat < 2 else AR2
                        wo_pre[bat] = b.dma("gpsimd", wo_buf[:, hsl_:hsl_ + 2048],
                                            wo[bat // 2][:, hsl_:hsl_ + 2048],
                                            "pj%d" % (bat // 2))
                    ms = None
                    for ti in range(16):
                        t_ = bat * 16 + ti
                        s_slot = bat * 16 + ti
                        for dl in range(4):
                            ms = b.ins("tensor", lambda e, t_=t_, dl=dl, pp=pp, ti=ti, s_slot=s_slot, Q=Q: e.matmul(
                                PS[dl * 32:dl * 32 + 32,
                                   2048 + pp * 512 + ti * 32:2048 + pp * 512 + ti * 32 + 32],
                                vv4[dl * 32:dl * 32 + 32, Q, :, t_],
                                av_rhs(Q, dl, s_slot),
                                start=True, stop=True, tile_position=(dl * 32, dl * 32)),
                                inc=("st", 1) if (ti == 15 and dl == 3) else None)
                    av_last_mm[Q] = ms
                    b.wait("vector", "st", ms)
                    dst3 = xln[:, :].rearrange("p (h t) -> p h t", t=T)[
                        :, :, Q * SQ + bat * 16:Q * SQ + bat * 16 + 16]
                    ev = b.ins("vector", lambda e, dst3=dst3, pp=pp: e.tensor_copy(
                        dst3,
                        PS[:, 2048 + pp * 512:2048 + pp * 512 + 512].rearrange(
                            "p (t h) -> p h t", h=32)), inc=("sv", 1))
                    av_ev[gi] = ev
                    if bat == NBAT - 1 and Q + 2 < NQ:
                        b.wait("gpsimd", "st", av_last_mm[Q])
                        rq_t[Q + 2] = rq_issue(Q + 2)
            b.barrier()
            proj_phase(xln, wo, 3, other, "o_evac", pre=(wo_pre[1], wo_pre[3]))

        vms0 = ln_phase(None, TA, True)
        if "ffn1" in phases:
            ffn_phase(TA, TB, final=False, xln_ready=vms0)
            vms1 = ln_phase(TB, TB, False)
        if "attn" in phases:
            attn_phase(TB, TA, xln_ready=vms1)
            vms2 = ln_phase(TA, TA, False)
        if "ffn2" in phases:
            ffn_phase(TA, TB, final=True, xln_ready=vms2)
        for s in list(b.sems):
            if s.startswith("ox") or s == "sp":
                b.wait("sync", s, b.cnt[s])

        @blk.sync
        def _(e):
            for f in b.ops["sync"]:
                f(e)

        @blk.scalar
        def _(e):
            for f in b.ops["scalar"]:
                f(e)

        @blk.vector
        def _(e):
            for f in b.ops["vector"]:
                f(e)

        @blk.tensor
        def _(e):
            for f in b.ops["tensor"]:
                f(e)

        @blk.gpsimd
        def _(e):
            for f in b.ops["gpsimd"]:
                f(e)

    return nc



def _tile_w(wT):
    w = wT.reshape(CH, 128, CH, 128)
    return np.ascontiguousarray(w.transpose(2, 1, 0, 3)).reshape(CH, 128, CH * 128)


def prepare_inputs(inp):
    g = {}
    Wq = np.asarray(inp["L0_attn_Wq"]); Wk = np.asarray(inp["L0_attn_Wk"])
    Wv = np.asarray(inp["L0_attn_Wv"]); Wo = np.asarray(inp["L0_attn_Wo"])
    g["wq"] = _tile_w(np.ascontiguousarray(Wq.T))
    g["wk"] = _tile_w(np.ascontiguousarray(Wk.T))
    perm_v = np.zeros(D, dtype=np.int64)
    for c in range(CH):
        for dl in range(4):
            for gg in range(32):
                perm_v[c * 128 + dl * 32 + gg] = gg * 128 + 4 * c + dl
    g["wv"] = _tile_w(np.ascontiguousarray(np.ascontiguousarray(Wv.T)[:, perm_v]))
    perm_o = np.zeros(D, dtype=np.int64)
    for h in range(H):
        for p in range(128):
            dl, dh = p // 32, p % 32
            perm_o[h * 128 + p] = h * 128 + 4 * dh + dl
    WoTp = np.ascontiguousarray(Wo.T)[perm_o, :]
    w = WoTp.reshape(CH, 128, CH, 128)
    g["wo"] = np.ascontiguousarray(w.transpose(2, 1, 0, 3)).reshape(CH, 128, CH * 128)
    g["bq"] = np.ascontiguousarray(np.asarray(inp["L0_attn_bq"]).reshape(CH, 128).T)
    g["bk"] = np.ascontiguousarray(np.asarray(inp["L0_attn_bk"]).reshape(CH, 128).T)
    g["bv"] = np.ascontiguousarray(np.asarray(inp["L0_attn_bv"])[perm_v].reshape(CH, 128).T)
    g["bo"] = np.ascontiguousarray(np.asarray(inp["L0_attn_bo"]).reshape(CH, 128).T)
    w1T = np.ascontiguousarray(np.asarray(inp["L0_ffn_w1"]).T)
    t = w1T.reshape(CH, 128, 28, 512)
    g["w1t"] = np.ascontiguousarray(t.transpose(2, 1, 0, 3)).reshape(28, 128, CH * 512)
    w2 = np.asarray(inp["L0_ffn_w2t"])
    t2 = w2.reshape(NGRP, JG, 128, CH, 128)
    g["w2g"] = np.ascontiguousarray(t2.transpose(0, 3, 2, 1, 4)).reshape(
        NGRP, CH, 128, JG * 128)
    c1T = np.ascontiguousarray(np.asarray(inp["L0_ffn_c1_w"]).T).reshape(CH, 128, RANK)
    g["c1t"] = np.ascontiguousarray(c1T.transpose(1, 0, 2)).reshape(128, CH * RANK)
    g["c2t"] = np.ascontiguousarray(np.asarray(inp["L0_ffn_c2_w"]).T)
    return g


def kernel(**inputs):
    x = np.asarray(inputs["L0_attn_x"])
    Bx, S, Dm = x.shape
    xt = x.reshape(Bx * S, Dm)
    g = prepare_inputs(inputs)
    T = (Bx * S) // NCORES
    nc = build_program(T=T)
    in_maps = []
    for c in range(NCORES):
        m = dict(g)
        m["x"] = np.ascontiguousarray(xt[c * T:(c + 1) * T, :].T)
        in_maps.append(m)
    res = run_bass_kernel_spmd(nc, in_maps, core_ids=list(range(NCORES)))
    outs = [np.asarray(res.results[c]["out"]).T for c in range(NCORES)]
    return np.concatenate(outs, axis=0).reshape(Bx, S, Dm).astype(np.float32)
